# version 17 (final)
# baseline (speedup 1.0000x reference)
import numpy as np
from contextlib import ExitStack

import concourse.bass as bass
import concourse.mybir as mybir
from concourse.bass_utils import run_bass_kernel_spmd

F32 = mybir.dt.float32
BF16 = mybir.dt.bfloat16
AF = mybir.ActivationFunctionType
ALU = mybir.AluOpType
AX = mybir.AxisListType

B, L, D = 2, 8192, 1024
H, DH, M = 16, 64, 64
T = 2048
NS, TS = 4, 512
KC = 8
NP = 8
NLT = 16
FF, NF = 4096, 32
NCORES = 8
SCALE = 0.125
INV_D = 1.0 / 1024.0
LN_EPS = 1e-5
B_SPIKE = [10000.0 / 40951.0, 19000.0 / 40951.0, 28100.0 / 40951.0,
           37100.0 / 40951.0, 44390.0 / 40951.0]

ENGINES = ("sync", "gpsimd", "tensor", "scalar", "vector")


class Plan:

    def __init__(self):
        self.steps = {e: [] for e in ENGINES}
        self.c = {}
        self.dma_fams = set()

    def wait(self, eng, sem, val):
        if val and val > 0:
            self.steps[eng].append(("w", sem, val))

    def inc(self, sem, n=1):
        self.c[sem] = self.c.get(sem, 0) + n
        return self.c[sem]

    def op(self, eng, fn, inc=None, inc_n=1):
        after = None
        if inc is not None:
            after = self.inc(inc, inc_n)
        self.steps[eng].append(("o", fn, inc, inc_n))
        return after

    def dma(self, eng, out, in_, fam):
        self.dma_fams.add(fam)
        return self.op(eng, lambda e, o=out, i=in_: e.dma_start(out=o, in_=i),
                       inc=fam, inc_n=16)

    def mm(self, out, lhsT, rhs, start, stop, inc=None, skip_gc=False):
        return self.op("tensor",
                       lambda e, o=out, l=lhsT, r=rhs, a=start, b=stop, g=skip_gc:
                       e.matmul(o, l, r, start=a, stop=b, skip_group_check=g),
                       inc=inc)

    def barrier(self, bar_name="bar", quiesce=True):
        tgt = self.c.get(bar_name, 0) + len(ENGINES)
        for eng in ENGINES:
            self.inc(bar_name)
            self.steps[eng].append(
                ("o", lambda e, d=self.sems_ref, n=bar_name: e.sem_inc(d[n], 1),
                 None, 1))
        for eng in ENGINES:
            self.wait(eng, bar_name, tgt)
            if quiesce:
                for fam in sorted(self.dma_fams):
                    self.wait(eng, fam, self.c.get(fam, 0))

    def flush_section(self, block, sems, nc=None):
        import contextlib
        cm = nc.allow_low_precision("bf16 softmax-weight operands") if nc \
            else contextlib.nullcontext()
        with cm:
            self._flush(block, sems)

    def _flush(self, block, sems):
        for eng in ENGINES:
            steps = self.steps[eng]
            self.steps[eng] = []

            def body(e, steps=steps):
                watermark = {}
                for st in steps:
                    if st[0] == "w":
                        _, sem, val = st
                        if watermark.get(sem, -1) >= val:
                            continue
                        e.wait_ge(sems[sem], val)
                        watermark[sem] = val
                    else:
                        _, fn, incsem, inc_n = st
                        inst = fn(e)
                        if incsem is not None:
                            inst.then_inc(sems[incsem], inc_n)

            getattr(block, eng)(body)


def build_nc(debug=False, paranoid=0, poison=False):
    nc = bass.Bass("TRN2", target_bir_lowering=False, debug=False)

    x_fm = nc.dram_tensor("x_fm", [D, T], F32, kind="ExternalInput").ap()
    wq_d = nc.dram_tensor("wq", [D, D], F32, kind="ExternalInput").ap()
    wkv_d = nc.dram_tensor("wkv", [D, 2 * D], F32, kind="ExternalInput").ap()
    wo_d = nc.dram_tensor("wo", [D, D], F32, kind="ExternalInput").ap()
    wfct_d = nc.dram_tensor("wfct", [D, FF], F32, kind="ExternalInput").ap()
    wprojt_d = nc.dram_tensor("wprojt", [FF, D], F32, kind="ExternalInput").ap()
    medt_d = nc.dram_tensor("medt", [D, M], F32, kind="ExternalInput").ap()
    bo_d = nc.dram_tensor("bo2d", [128, KC], F32, kind="ExternalInput").ap()
    ind_d = nc.dram_tensor("ind_c", [128, 2], F32, kind="ExternalInput").ap()
    indt_d = nc.dram_tensor("indt_c", [2, 128], F32, kind="ExternalInput").ap()
    pones_d = nc.dram_tensor("pones_c", [128, 128], F32, kind="ExternalInput").ap()
    out_d = nc.dram_tensor("out_fm", [D, T], F32, kind="ExternalOutput").ap()

    xn_scr = nc.dram_tensor("xn_scr", [NS, 128, KC, TS], BF16).ap()
    e2n_scr = nc.dram_tensor("e2n_scr", [NS, 128, NP, TS], BF16).ap()
    acc_scr = nc.dram_tensor("acc_scr", [NS, 128, KC, TS], BF16).ap()
    ccin = nc.dram_tensor("ccin", [129, 1024], F32)
    ccout = nc.dram_tensor("ccout", [129, 1024], F32)

    dbg = {}
    if debug:
        for nm, shp in [("dbg_xn", [128, KC, TS]), ("dbg_k", [128, T]),
                        ("dbg_accT", [128, 1024]), ("dbg_sume", [1, 1024]),
                        ("dbg_mkv", [128, 1024]), ("dbg_g", [128, KC, 1024]),
                        ("dbg_e2n", [128, NP, TS]), ("dbg_q", [128, KC, TS]),
                        ("dbg_y", [128, TS]), ("dbg_acc", [128, KC, TS]),
                        ("dbg_xnf", [128, KC, TS])]:
            dbg[nm] = nc.dram_tensor(nm, shp, F32, kind="ExternalOutput").ap()
        dbg["dbg_h"] = nc.dram_tensor("dbg_h", [128, 4, TS], BF16,
                                      kind="ExternalOutput").ap()

    p = Plan()

    def pb(level=1):
        if level in ("a2a", "a2b", "s2", "ffn", "s3e"):
            p.barrier(quiesce=False)
            return
        if isinstance(paranoid, str):
            if str(level) in paranoid.split(","):
                p.barrier()
        elif paranoid >= (level if isinstance(level, int) else 99):
            p.barrier()
    stk = ExitStack()

    def sb(name, shape, dt):
        return stk.enter_context(nc.sbuf_tensor(name, shape, dt))

    sem_names = ["t", "sc", "v", "gc", "cc", "bar",
                 "dpsn", "dset", "dwkv", "dwq", "dx0", "dx1", "dxnw", "ddbg",
                 "dcc", "dwp", "dwf", "dxq0", "dxq1", "dew", "dcb", "dwo0",
                 "dwo1", "dxnf", "dac0", "dac1", "dld0", "dld1", "dout0",
                 "dout1", "dout2", "dout3"]
    sems = {n: stk.enter_context(nc.semaphore(n)) for n in sem_names}
    p.sems_ref = sems

    ones_bf = sb("ones_bf", [128, 128], BF16)
    med_blk = sb("med_blk", [128, NP, 128], BF16)
    ind = sb("ind", [128, 2], BF16)
    indT = sb("indT", [2, 128], BF16)
    pones = sb("pones", [128, 128], BF16)
    bo_sb = sb("bo_sb", [128, KC], F32)

    psA = stk.enter_context(nc.psum_tensor("psA", [128, 1024], F32))
    psB = stk.enter_context(nc.psum_tensor("psB", [128, 1024], F32))
    psC = stk.enter_context(nc.psum_tensor("psC", [128, 1024], F32))
    psD = stk.enter_context(nc.psum_tensor("psD", [128, 1024], F32))

    st_wq = ExitStack()
    wq_sb = st_wq.enter_context(
        nc.sbuf_tensor("wq_sb", [128, KC, D], BF16, side="right"))
    q_sb = st_wq.enter_context(
        nc.sbuf_tensor("q_sb", [128, KC, T], BF16, side="right"))

    st_A = ExitStack()

    def sbA(name, shape, dt):
        return st_A.enter_context(nc.sbuf_tensor(name, shape, dt))

    wkv_sb = sbA("wkv_sb", [128, KC, 2 * D], BF16)
    xn_sb = sbA("xn_sb", [128, KC, T], BF16)
    k_sb = sbA("k_sb", [128, KC, T], BF16)
    x_sl = sbA("x_sl", [128, 2, KC, TS], BF16)
    x2_sl = sbA("x2_sl", [128, KC, TS], BF16)
    mstat = sbA("mstat", [128, TS], BF16)
    rstat = sbA("rstat", [128, TS], BF16)
    m2f = sbA("m2f", [128, TS], F32)
    varf = sbA("varf", [128, TS], F32)
    rinvf = sbA("rinvf", [128, TS], F32)
    xcb = sbA("xcb", [128, TS], BF16)
    v_rot = sbA("v_rot", [128, 2, 1024], BF16)
    e_rot = sbA("e_rot", [128, 2, 1024], BF16)
    med_stage = sbA("med_stage", [128, KC, M], F32)
    acc_stage = sbA("acc_stage", [128, 1024], F32)
    sume_stage = sbA("sume_stage", [1, 1024], F32)

    poison_done = 0
    if poison:
        with nc.sbuf_tensor("psn", [128, KC * TS], BF16) as psn:
            p.op("gpsimd", lambda e: e.memset(psn[:, :], 1e30), inc="gc")
            for si in range(NS):
                p.dma("gpsimd", xn_scr[si, :, :, :], psn[:, :].rearrange(
                    "p (c t) -> p c t", c=KC), "dpsn")
                p.dma("gpsimd", e2n_scr[si, :, :, :], psn[:, :].rearrange(
                    "p (c t) -> p c t", c=KC), "dpsn")
                poison_done = p.dma("gpsimd", acc_scr[si, :, :, :],
                                    psn[:, :].rearrange("p (c t) -> p c t", c=KC),
                                    "dpsn")
            p.dma("gpsimd", ccin.ap()[0:128, :], psn[:, 0:1024].rearrange(
                "p t -> p t"), "dpsn")
            poison_done = p.dma("gpsimd", ccout.ap()[0:128, :],
                                psn[:, 0:1024].rearrange("p t -> p t"), "dpsn")
    g_med = p.dma("gpsimd", med_stage[:, :, :],
                  medt_d.rearrange("(c p) m -> p c m", p=128), "dset")
    g_bo = p.dma("gpsimd", bo_sb[:, :], bo_d[:, :], "dset")
    p.op("gpsimd", lambda e: e.memset(ones_bf[:, :], 1.0), inc="gc")
    gc_setup = p.op("gpsimd", lambda e: e.memset(med_blk[:, :, :], 0.0), inc="gc")
    p.dma("gpsimd", ind[:, :], ind_d, "dset")
    p.dma("gpsimd", indT[:, :], indt_d, "dset")
    g_ind = p.dma("gpsimd", pones[:, :], pones_d, "dset")

    p.wait("scalar", "dset", g_ind)
    p.wait("scalar", "gc", gc_setup)
    med_ready = None
    for pp in range(NP):
        p.op("scalar", lambda e, pp=pp: e.activation(
            med_blk[0:64, pp, 0:64], med_stage[0:64, pp, :], AF.Copy, scale=SCALE))
        med_ready = p.op("scalar", lambda e, pp=pp: e.activation(
            med_blk[64:128, pp, 64:128], med_stage[64:128, pp, :], AF.Copy,
            scale=SCALE), inc="sc")

    wkv_ready = p.dma("gpsimd", wkv_sb[:, :, :],
                      wkv_d.rearrange("(c p) n -> p c n", p=128), "dwkv")
    wq_ready = p.dma("gpsimd", wq_sb[:, :, :],
                     wq_d.rearrange("(c p) n -> p c n", p=128), "dwq")

    xs_ready = [None] * NS
    x2_ready = [None] * NS
    stats_done = [None] * NS
    ln_v_done = [None] * NS
    rstd_done = [None] * NS
    xn_done = [None] * NS
    xnscr_done = [None] * NS
    p.wait("tensor", "gc", gc_setup)
    p.wait("tensor", "dset", g_ind)

    def a1_slice(s):
        if s >= 2:
            p.wait("gpsimd", "v", xn_done[s - 2])
        xs_ready[s] = p.dma(
            "gpsimd", x_sl[:, s % 2, :, :],
            x_fm.rearrange("(c p) t -> p c t", p=128)[:, :, s * TS:(s + 1) * TS],
            "dx%d" % (s % 2))
        p.wait("scalar", "dx%d" % (s % 2), xs_ready[s])
        if s >= 1:
            p.wait("scalar", "t", stats_done[s - 1])
        x2_ready[s] = p.op("scalar", lambda e, s=s: e.activation(
            x2_sl[:, :, :], x_sl[:, s % 2, :, :], AF.Square), inc="sc")
        p.wait("tensor", "sc", x2_ready[s])
        if s >= 1:
            p.wait("tensor", "v", ln_v_done[s - 1])
        for kk in range(KC):
            p.mm(psD[:, 0:TS], ones_bf[:, :], x_sl[:, s % 2, kk, :],
                 start=(kk == 0), stop=(kk == KC - 1))
        for kk in range(KC):
            stats_done[s] = p.mm(psD[:, TS:2 * TS], ones_bf[:, :], x2_sl[:, kk, :],
                                 start=(kk == 0), stop=(kk == KC - 1),
                                 inc="t" if kk == KC - 1 else None)
        p.wait("vector", "t", stats_done[s])
        if s >= 1:
            p.wait("vector", "v", xn_done[s - 1])
        c1 = p.op("vector", lambda e: e.tensor_scalar_mul(
            mstat[:, :], psD[:, 0:TS], INV_D), inc="v")
        p.wait("vector", "v", c1)
        c2 = p.op("vector", lambda e: e.tensor_tensor(
            m2f[:, :], mstat[:, :], mstat[:, :], op=ALU.mult), inc="v")
        p.wait("vector", "v", c2)
        c3 = p.op("vector", lambda e: e.scalar_tensor_tensor(
            varf[:, :], psD[:, TS:2 * TS], INV_D, m2f[:, :],
            op0=ALU.mult, op1=ALU.subtract), inc="v")
        p.wait("vector", "v", c3)
        c4 = p.op("vector", lambda e: e.tensor_scalar_add(
            varf[:, :], varf[:, :], LN_EPS), inc="v")
        p.wait("vector", "v", c4)
        ln_v_done[s] = p.op("vector", lambda e: e.reciprocal(rinvf[:, :], varf[:, :]),
                            inc="v")
        p.wait("scalar", "v", ln_v_done[s])
        rstd_done[s] = p.op("scalar", lambda e: e.activation(
            rstat[:, :], rinvf[:, :], AF.Sqrt), inc="sc")
        p.wait("vector", "sc", rstd_done[s])
        if s >= 1:
            p.wait("vector", "dxnw", xnscr_done[s - 1])
        prev_mul = None
        for kk in range(KC):
            if prev_mul is not None:
                p.wait("vector", "v", prev_mul)
            ck = p.op("vector", lambda e, s=s, kk=kk: e.tensor_tensor(
                xcb[:, :], x_sl[:, s % 2, kk, :], mstat[:, :], op=ALU.subtract),
                inc="v")
            p.wait("vector", "v", ck)
            xn_done[s] = prev_mul = p.op(
                "vector", lambda e, s=s, kk=kk: e.tensor_tensor(
                    xn_sb[:, kk, s * TS:(s + 1) * TS], xcb[:, :], rstat[:, :],
                    op=ALU.mult), inc="v")
        p.wait("sync", "v", xn_done[s])
        p.wait("sync", "dpsn", poison_done)
        xnscr_done[s] = p.dma("sync", xn_scr[s, :, :, :], fam="dxnw", in_=
                              xn_sb[:, :, s * TS:(s + 1) * TS])

    if debug:
        p.wait("gpsimd", "v", xn_done[0])
        p.dma("gpsimd", dbg["dbg_xn"], xn_sb[:, :, 0:TS], "ddbg")

    pb("a2a")
    pK6 = [psA[:, 0:TS], psA[:, TS:2 * TS], psB[:, 0:TS], psB[:, TS:2 * TS],
           psC[:, 0:TS], psC[:, TS:2 * TS]]
    k_evac = [[None] * KC for _ in range(NS)]
    q_evac = [[None] * KC for _ in range(NS)]
    ring_evac = []
    k_grp = 0

    def proj_slice(s4, w_sb, w_sem, w_val, dst, evac):
        nonlocal k_grp
        p.wait("tensor", "v", xn_done[s4])
        p.wait("tensor", w_sem, w_val)
        for n in range(KC):
            tile = pK6[k_grp % 6]
            prev = k_grp - 6
            if prev >= 0:
                p.wait("tensor", "sc", ring_evac[prev])
            kmm = None
            for kk in range(KC):
                kmm = p.mm(tile, w_sb[:, kk, n * 128:(n + 1) * 128],
                           xn_sb[:, kk, s4 * TS:(s4 + 1) * TS],
                           start=(kk == 0), stop=(kk == KC - 1),
                           inc="t" if kk == KC - 1 else None)
            p.wait("scalar", "t", kmm)
            evac[s4][n] = p.op("scalar", lambda e, n=n, s4=s4, tile=tile, d=dst:
                               e.copy(d[:, n, s4 * TS:(s4 + 1) * TS], tile),
                               inc="sc")
            ring_evac.append(evac[s4][n])
            k_grp += 1

    def a2a_slice(s4):
        proj_slice(s4, wkv_sb, "dwkv", wkv_ready, k_sb, k_evac)

    def a2q_slice(s4):
        proj_slice(s4, wq_sb, "dwq", wq_ready, q_sb, q_evac)

    a1_slice(0)
    for _s in range(1, NS):
        a1_slice(_s)
        a2a_slice(_s - 1)
        a2q_slice(_s - 1)
    a2a_slice(NS - 1)
    a2q_slice(NS - 1)

    if debug:
        p.wait("gpsimd", "sc", k_evac[NS - 1][0])
        p.dma("gpsimd", dbg["dbg_k"], k_sb[:, 0, :], "ddbg")

    pb("a2b")
    p.wait("vector", "sc", ring_evac[-1])
    p.wait("tensor", "sc", ring_evac[-1])
    p.op("vector", lambda e: e.memset(psC[:, :], 0.0))
    accz = p.op("vector", lambda e: e.memset(psD[:, :], 0.0), inc="v")
    p.wait("tensor", "v", accz)
    p.wait("tensor", "sc", med_ready)
    v_mm = [None] * NLT
    s_mm = [None] * NLT
    v_evac = [None] * NLT
    e_done = [None] * NLT
    acc_prog = [None] * NLT
    for lt in range(NLT):
        if lt >= 1:
            p.wait("tensor", "sc", e_done[lt - 1])
        for pp in range(NP):
            s_mm[lt] = p.mm(psB[:, pp * 128:(pp + 1) * 128],
                            k_sb[:, pp, lt * 128:(lt + 1) * 128],
                            med_blk[:, pp, :], start=True, stop=True,
                            inc="t" if pp == NP - 1 else None)
        p.wait("scalar", "t", s_mm[lt])
        e_done[lt] = p.op("scalar", lambda e, lt=lt: e.activation(
            e_rot[:, lt % 2, :], psB[:, :], AF.Exp), inc="sc")
        if lt >= 1:
            p.wait("tensor", "v", v_evac[lt - 1])
        for kk in range(KC):
            lhs = xn_sb[:, kk, lt * 128:(lt + 1) * 128]
            p.mm(psA[:, 0:512], lhs, wkv_sb[:, kk, 1024:1536],
                 start=(kk == 0), stop=(kk == KC - 1))
            v_mm[lt] = p.mm(psA[:, 512:1024], lhs, wkv_sb[:, kk, 1536:2048],
                            start=(kk == 0), stop=(kk == KC - 1),
                            inc="t" if kk == KC - 1 else None)
        p.wait("vector", "t", v_mm[lt])
        v_evac[lt] = p.op("vector", lambda e, lt=lt: e.tensor_copy(
            v_rot[:, lt % 2, :], psA[:, :]), inc="v")
        p.wait("tensor", "sc", e_done[lt])
        p.wait("tensor", "v", v_evac[lt])
        for pp in range(NP):
            p.mm(psC[:, pp * 128:(pp + 1) * 128], v_rot[:, lt % 2, pp * 128:(pp + 1) * 128],
                 e_rot[:, lt % 2, pp * 128:(pp + 1) * 128],
                 start=False, stop=(lt == NLT - 1), skip_gc=True)
        for pp in range(NP):
            acc_prog[lt] = p.mm(psD[0:1, pp * 128:(pp + 1) * 128], ones_bf[:, 0:1],
                                e_rot[:, lt % 2, pp * 128:(pp + 1) * 128],
                                start=False, stop=(lt == NLT - 1),
                                inc="t" if pp == NP - 1 else None, skip_gc=True)

    p.wait("scalar", "t", acc_prog[NLT - 1])
    p.op("scalar", lambda e: e.copy(acc_stage[:, :], psC[:, :]))
    stage_done = p.op("scalar", lambda e: e.copy(sume_stage[:, :], psD[0:1, :]),
                      inc="sc")
    p.wait("gpsimd", "sc", stage_done)
    g1 = p.dma("gpsimd", ccin.ap()[0:128, :], acc_stage[:, :], "dcc")
    g2 = p.dma("gpsimd", ccin.ap()[128:129, :], sume_stage[:, :], "dcc")


    if debug:
        p.wait("sync", "sc", stage_done)
        p.dma("sync", dbg["dbg_accT"], acc_stage[:, :], "ddbg")
        p.dma("sync", dbg["dbg_sume"], sume_stage[:, :], "ddbg")

    p.barrier()
    with nc.Block("s1") as blk1:
        p.flush_section(blk1, sems, nc)
    st_A.close()

    st_W2 = ExitStack()
    wprojt_sb = st_W2.enter_context(nc.sbuf_tensor("wprojt_sb", [128, NF, D], BF16))
    wfct_sb = st_W2.enter_context(nc.sbuf_tensor("wfct_sb", [128, KC, FF], BF16))
    st_S2 = ExitStack()

    def sb2(name, shape, dt):
        return st_S2.enter_context(nc.sbuf_tensor(name, shape, dt))

    e2_sb = sb2("e2_sb", [128, NP, TS], BF16)
    e2n_sb = sb2("e2n_sb", [128, NP, TS], BF16)
    recb = sb2("recb", [128, 2, TS], BF16)

    p.wait("gpsimd", "dcc", g2)
    p.op("gpsimd", lambda e: e.collective_compute(
        "AllReduce", ALU.add, replica_groups=[[0, 1, 2, 3], [4, 5, 6, 7]],
        ins=[ccin.ap().opt()], outs=[ccout.ap().opt()]), inc="cc")

    wproj_ready = None
    for g4 in range(4):
        wproj_ready = p.dma(
            "gpsimd", wprojt_sb[:, g4 * 8:(g4 + 1) * 8, :],
            wprojt_d.rearrange("(c p) n -> p c n", p=128)[:, g4 * 8:(g4 + 1) * 8, :],
            "dwp")
    wfct_ready = [None] * KC
    for g8 in range(KC):
        wfct_ready[g8] = p.dma(
            "gpsimd", wfct_sb[:, :, g8 * 512:(g8 + 1) * 512],
            wfct_d.rearrange("(c p) f -> p c f", p=128)[:, :, g8 * 512:(g8 + 1) * 512],
            "dwf")

    s2_mm = [[None] * NP for _ in range(NS)]
    e2_done = [[None] * NP for _ in range(NS)]
    sum_mm = [[None] * NP for _ in range(NS)]
    rec_done = [[None] * NP for _ in range(NS)]
    bc_mm = [[None] * NP for _ in range(NS)]
    e2n_done = [[None] * NP for _ in range(NS)]
    e2nscr_done = [None] * NS
    if True:
        s2ring = [psB[:, 0:TS], psB[:, TS:2 * TS], psD[:, 0:TS], psD[:, TS:2 * TS]]
        smring = [psC[:, 0:TS], psC[:, TS:2 * TS], psA[:, 0:TS], psA[:, TS:2 * TS]]

        def em_s2(s, pp):
            g = s * NP + pp
            p.wait("tensor", "sc", q_evac[s][pp])
            p.wait("tensor", "sc", med_ready)
            if g >= 4:
                p.wait("tensor", "sc", e2_done[(g - 4) // NP][(g - 4) % NP])
            s2_mm[s][pp] = p.mm(s2ring[g % 4],
                                med_blk[:, pp, :], q_sb[:, pp, s * TS:(s + 1) * TS],
                                start=True, stop=True, inc="t")

        def em_exp(s, pp):
            g = s * NP + pp
            p.wait("scalar", "t", s2_mm[s][pp])
            if s > 0:
                p.wait("scalar", "t", bc_mm[s - 1][pp])
                p.wait("scalar", "v", e2n_done[s - 1][pp])
            e2_done[s][pp] = p.op("scalar", lambda e, s=s, pp=pp, g=g: e.activation(
                e2_sb[:, pp, :], s2ring[g % 4], AF.Exp),
                inc="sc")

        def em_sumb(s, pp):
            g = s * NP + pp
            p.wait("tensor", "sc", e2_done[s][pp])
            p.wait("tensor", "gc", gc_setup)
            if g >= 4:
                p.wait("tensor", "v", e2n_done[(g - 4) // NP][(g - 4) % NP])
            bc_mm[s][pp] = p.mm(smring[g % 4],
                                pones[:, :], e2_sb[:, pp, :], start=True, stop=True,
                                inc="t")

        def em_div(s, pp):
            g = s * NP + pp
            p.wait("vector", "t", bc_mm[s][pp])
            rg = p.op("vector", lambda e, pp=pp, g=g: e.reciprocal(
                recb[:, pp % 2, :], smring[g % 4]),
                inc="v")
            p.wait("vector", "v", rg)
            p.wait("vector", "sc", e2_done[s][pp])
            if s >= 1:
                p.wait("vector", "dew", e2nscr_done[s - 1])
            e2n_done[s][pp] = p.op("vector", lambda e, s=s, pp=pp: e.tensor_tensor(
                e2n_sb[:, pp, :], e2_sb[:, pp, :],
                recb[:, pp % 2, :], op=ALU.mult), inc="v")

    def s2_qm_tail(s):
        p.wait("sync", "v", e2n_done[s][NP - 1])
        e2nscr_done[s] = p.dma("sync", e2n_scr[s, :, :, :], e2n_sb[:, :, :],
                               "dew")
        if debug and s == 0:
            p.wait("gpsimd", "v", e2n_done[0][NP - 1])
            p.dma("gpsimd", dbg["dbg_e2n"], e2n_sb[:, :, :], "ddbg")
            p.wait("gpsimd", "sc", q_evac[0][KC - 1])
            dq = p.dma("gpsimd", dbg["dbg_q"], q_sb[:, :, 0:TS], "ddbg")
            p.wait("scalar", "ddbg", dq)
            p.wait("vector", "ddbg", dq)

    UNITS = NS * NP

    def unit(g):
        return g // NP, g % NP

    for g in range(UNITS + 2):
        if g < UNITS:
            em_s2(*unit(g))
            em_exp(*unit(g))
        if g >= 2:
            su, pu = unit(g - 2)
            em_sumb(su, pu)
            em_div(su, pu)
            if pu == NP - 1:
                s2_qm_tail(su)

    p.barrier()
    with nc.Block("s2") as blk2:
        p.flush_section(blk2, sems, nc)
    st_S2.close()
    st_wq.close()

    st_G = ExitStack()
    G_sb = st_G.enter_context(nc.sbuf_tensor("G_sb", [128, KC, D], BF16,
                                             side="right"))
    mkv_sb = st_G.enter_context(nc.sbuf_tensor("mkv_sb", [128, 1024], BF16,
                                               side="right"))
    ccb_acc = st_G.enter_context(nc.sbuf_tensor("ccb_acc", [128, 1024], BF16,
                                                side="right"))
    ccb_sume = st_G.enter_context(nc.sbuf_tensor("ccb_sume", [1, 1024], BF16,
                                                 side="right"))
    srecip = st_G.enter_context(nc.sbuf_tensor("srecip", [128, 1024], BF16,
                                               side="right"))
    st_F = ExitStack()

    def sbF(name, shape, dt):
        return st_F.enter_context(nc.sbuf_tensor(name, shape, dt))

    wo_rot = sbF("wo_rot", [128, 2, D], BF16)
    xn_f = sbF("xn_f", [128, KC, TS], BF16)
    h_sb = sbF("h_sb", [128, NF, TS], BF16)
    hrelu = sbF("hrelu", [128, TS], BF16)
    accs_sb = sbF("accs_sb", [128, 2, TS], BF16)
    sp1 = sbF("sp1", [128, TS], BF16)
    sp2 = sbF("sp2", [128, TS], BF16)
    dbgy_buf = sbF("dbgy_buf", [128, TS], F32) if debug else None

    p.wait("gpsimd", "cc", 1)
    p.dma("gpsimd", ccb_acc[:, :], ccout.ap()[0:128, :], "dcb")
    ccback = p.dma("gpsimd", ccb_sume[:, :], ccout.ap()[128:129, :], "dcb")
    g_mm = [None] * H
    g_evac = [None] * H
    g_state = {}
    sr_state = {}

    def prelude_b(relu31, hmul31):
        p.wait("tensor", "dcb", ccback)
        p.mm(psD[:, 0:512], ones_bf[0:1, :], ccb_sume[0:1, 0:512],
             start=True, stop=True)
        bc2 = p.mm(psD[:, 512:1024], ones_bf[0:1, :], ccb_sume[0:1, 512:1024],
                   start=True, stop=True, inc="t")
        p.wait("vector", "t", bc2)
        sr_done = p.op("vector", lambda e: e.reciprocal(srecip[:, :], psD[:, :]),
                       inc="v")
        sr_state["sr"] = sr_done
        p.wait("vector", "v", sr_done)
        mkv_done = p.op("vector", lambda e: e.tensor_tensor(
            mkv_sb[:, :], ccb_acc[:, :], srecip[:, :], op=ALU.mult), inc="v")
        p.wait("tensor", "sc", relu31)
        p.wait("tensor", "v", hmul31)
        for h in range(H):
            pp, off = h // 2, 64 * (h % 2)
            if h >= 2:
                p.wait("gpsimd", "t", g_mm[h - 2])
            woh = p.dma("gpsimd", wo_rot[off:off + 64, h % 2, :],
                        wo_d[h * 64:(h + 1) * 64, :], "dwo%d" % (h % 2))
            p.wait("tensor", "v", mkv_done)
            p.wait("tensor", "dwo%d" % (h % 2), woh)
            if h >= 2:
                p.wait("tensor", "sc", g_evac[h - 2])
            ps = psB if h % 2 == 0 else psC
            lhs = mkv_sb[off:off + 64, pp * 128 + off:pp * 128 + off + 64]
            p.mm(ps[0:64, 0:512], lhs, wo_rot[off:off + 64, h % 2, 0:512],
                 start=True, stop=True)
            g_mm[h] = p.mm(ps[0:64, 512:1024], lhs,
                           wo_rot[off:off + 64, h % 2, 512:1024],
                           start=True, stop=True, inc="t")
            p.wait("scalar", "t", g_mm[h])
            g_evac[h] = p.op("scalar", lambda e, h=h, pp=pp, off=off, ps=ps: e.copy(
                G_sb[off:off + 64, pp, :], ps[0:64, :]), inc="sc")
        g_state["g_all"] = g_evac[H - 1]
        if debug:
            p.wait("gpsimd", "v", mkv_done)
            p.dma("gpsimd", dbg["dbg_mkv"], mkv_sb[:, :], "ddbg")
            p.wait("gpsimd", "sc", g_state["g_all"])
            p.dma("gpsimd", dbg["dbg_g"], G_sb[:, :, :], "ddbg")

    xnf_ready = [None] * NS
    fc_all = [None] * NS
    hmul = [[None] * NF for _ in range(NS)]
    relu_done = [[None] * NF for _ in range(NS)]
    proj_mm = [[None] * KC for _ in range(NS)]
    spike_done = [[None] * KC for _ in range(NS)]
    accdma = [[None] * KC for _ in range(NS)]
    pH = [psA[:, 0:512], psA[:, 512:1024], psB[:, 0:512], psB[:, 512:1024]]
    for s in range(NS):
        p.wait("sync", "dxnw", xnscr_done[NS - 1])
        if s >= 1:
            p.wait("sync", "t", fc_all[s - 1])
        xnf_ready[s] = p.dma("sync", xn_f[:, :, :], xn_scr[s, :, :, :], "dxnf")
        if debug and s == 0:
            p.wait("gpsimd", "dxnf", xnf_ready[0])
            dxnf = p.dma("gpsimd", dbg["dbg_xnf"], xn_f[:, :, :], "ddbg")
        if debug and s == 1:
            p.wait("sync", "ddbg", dxnf)
        for g2 in range(NF // 2):
            f0 = 2 * g2
            p.wait("tensor", "dwf", wfct_ready[KC - 1])
            if g2 >= 2:
                p.wait("tensor", "v", hmul[s][2 * (g2 - 2) + 1])
            elif s >= 1:
                p.wait("tensor", "v", hmul[s - 1][2 * (g2 - 2 + NF // 2) + 1])
            p.wait("tensor", "dxnf", xnf_ready[s])
            for kk in range(KC):
                for fi in range(2):
                    f = f0 + fi
                    fc_all[s] = p.mm(pH[2 * (g2 % 2) + fi],
                                     wfct_sb[:, kk, f * 128:(f + 1) * 128],
                                     xn_f[:, kk, :], start=(kk == 0),
                                     stop=(kk == KC - 1),
                                     inc="t" if (kk == KC - 1 and fi == 1) else None)
            for fi in range(2):
                f = f0 + fi
                p.wait("scalar", "t", fc_all[s])
                if fi == 1:
                    p.wait("scalar", "v", hmul[s][f - 1])
                elif g2 >= 1:
                    p.wait("scalar", "v", hmul[s][f0 - 1])
                elif s >= 1:
                    p.wait("scalar", "v", hmul[s - 1][NF - 1])
                relu_done[s][f] = p.op(
                    "scalar", lambda e, ps=pH[2 * (g2 % 2) + fi]: e.activation(
                        hrelu[:, :], ps, AF.Relu), inc="sc")
                p.wait("vector", "sc", relu_done[s][f])
                hmul[s][f] = p.op(
                    "vector", lambda e, s=s, f=f, ps=pH[2 * (g2 % 2) + fi]:
                    e.tensor_tensor(h_sb[:, f, :], ps, hrelu[:, :], op=ALU.mult),
                    inc="v")
        pb(2)
        if s == 0:
            prelude_b(relu_done[0][NF - 1], hmul[0][NF - 1])
        pyring = [psC[:, 0:TS], psC[:, TS:2 * TS], psD[:, 0:TS], psD[:, TS:2 * TS]]
        for n in range(KC):
            gg = s * KC + n
            p.wait("tensor", "v", hmul[s][NF - 1])
            p.wait("tensor", "dwp", wproj_ready)
            if s == 0 and n == 0:
                p.wait("tensor", "sc", g_state["g_all"])
            if s == 0 and n == 2:
                p.wait("tensor", "v", sr_state["sr"])
            if gg >= 4:
                pv = spike_done[(gg - 4) // KC][(gg - 4) % KC]
                p.wait("tensor", "v", pv)
            for ff in range(NF):
                proj_mm[s][n] = p.mm(pyring[gg % 4],
                                     wprojt_sb[:, ff, n * 128:(n + 1) * 128],
                                     h_sb[:, ff, :], start=(ff == 0),
                                     stop=(ff == NF - 1),
                                     inc="t" if ff == NF - 1 else None)
            p.wait("vector", "t", proj_mm[s][n])
            if debug and s == 0 and n == 0:
                dyv = p.op("vector", lambda e: e.tensor_copy(dbgy_buf[:, :],
                                                             psC[:, 0:TS]), inc="v")
                p.wait("sync", "v", dyv)
                p.dma("sync", dbg["dbg_y"], dbgy_buf[:, :], "ddbg")
                p.wait("sync", "v", hmul[0][NF - 1])
                for _fc in range(4):
                    p.dma("sync", dbg["dbg_h"][:, _fc, :], h_sb[:, _fc, :], "ddbg")
            if s + n > 0:
                pv = accdma[s][n - 2] if n >= 2 else accdma[s - 1][KC - 2 + n]
                p.wait("vector", "dac%d" % (n % 2), pv)
            yps = pyring[gg % 4]
            ck = p.op("vector", lambda e, yps=yps: e.tensor_single_scalar(
                sp1[:, :], yps, B_SPIKE[0], op=ALU.is_gt), inc="v")
            for bi, (dst, src) in enumerate([(sp2, sp1), (sp1, sp2), (sp2, sp1)]):
                p.wait("vector", "v", ck)
                ck = p.op("vector", lambda e, yps=yps, d=dst, r=src, b=B_SPIKE[bi + 1]:
                          e.scalar_tensor_tensor(d[:, :], yps, b, r[:, :],
                                                 op0=ALU.is_gt, op1=ALU.add), inc="v")
            p.wait("vector", "v", ck)
            spike_done[s][n] = p.op(
                "vector", lambda e, yps=yps, n=n: e.scalar_tensor_tensor(
                    accs_sb[:, n % 2, :], yps, B_SPIKE[4], sp2[:, :],
                    op0=ALU.is_gt, op1=ALU.add), inc="v")
            p.wait("gpsimd", "v", spike_done[s][n])
            accdma[s][n] = p.dma("gpsimd", acc_scr[s, :, n, :],
                                 accs_sb[:, n % 2, :], "dac%d" % (n % 2))
            pb(3)
    pb("s3e")
    p.barrier()
    with nc.Block("s3") as blk3:
        p.flush_section(blk3, sems, nc)
    st_F.close()
    st_W2.close()

    st_C = ExitStack()

    def sbC(name, shape, dt):
        return st_C.enter_context(nc.sbuf_tensor(name, shape, dt))

    e2n_sl = sbC("e2n_sl", [128, 2, NP, TS], BF16)
    xr_sl = sbC("xr_sl", [128, 2, KC, TS], F32)
    accr_sl = sbC("accr_sl", [128, 2, KC, TS], BF16)
    t1b = sbC("t1b", [128, 4, TS], F32)
    t2b = sbC("t2b", [128, 4, TS], F32)
    outb = sbC("outb", [128, 4, TS], F32)

    attn_mm = [[None] * KC for _ in range(NS)]
    t1_done = [[None] * KC for _ in range(NS)]
    t2_done = [[None] * KC for _ in range(NS)]
    out_done = [[None] * KC for _ in range(NS)]
    outdma = [[None] * KC for _ in range(NS)]
    cin = [None] * NS
    p.wait("sync", "dew", e2nscr_done[NS - 1])

    def s4_loads(u):
        if u >= 2:
            p.wait("sync", "t", attn_mm[u - 2][KC - 1])
            p.wait("sync", "v", out_done[u - 2][KC - 1])
        p.wait("sync", "dac0", accdma[u][KC - 2])
        p.wait("sync", "dac1", accdma[u][KC - 1])
        fam = "dld%d" % (u % 2)
        p.dma("sync", e2n_sl[:, u % 2, :, :], e2n_scr[u, :, :, :], fam)
        p.dma("sync", xr_sl[:, u % 2, :, :],
              x_fm.rearrange("(c p) t -> p c t", p=128)[:, :, u * TS:(u + 1) * TS],
              fam)
        cin[u] = p.dma("sync", accr_sl[:, u % 2, :, :], acc_scr[u, :, :, :], fam)

    for s in range(NS):
        if s == 0:
            s4_loads(0)
        if s + 1 < NS:
            s4_loads(s + 1)
        if debug and s == 0:
            p.wait("gpsimd", "dld0", cin[0])
            dacc = p.dma("gpsimd", dbg["dbg_acc"], accr_sl[:, 0, :, :], "ddbg")
            p.wait("sync", "ddbg", dacc)
        p.wait("tensor", "dld%d" % (s % 2), cin[s])
        aring = [psA[:, 0:TS], psA[:, TS:2 * TS], psB[:, 0:TS], psB[:, TS:2 * TS]]
        for n in range(KC):
            gg = s * KC + n

            def back(arr, k):
                return arr[(gg - k) // KC][(gg - k) % KC] if gg >= k else None
            p.wait("tensor", "sc", back(t1_done, 4))
            for kk in range(KC):
                attn_mm[s][n] = p.mm(aring[gg % 4],
                                     G_sb[:, kk, n * 128:(n + 1) * 128],
                                     e2n_sl[:, s % 2, kk, :], start=(kk == 0),
                                     stop=(kk == KC - 1),
                                     inc="t" if kk == KC - 1 else None)
            p.wait("scalar", "t", attn_mm[s][n])
            p.wait("scalar", "v", back(t2_done, 4))
            t1_done[s][n] = p.op("scalar", lambda e, s=s, n=n, gg=gg: e.activation(
                t1b[:, gg % 4, :], aring[gg % 4], AF.Identity,
                bias=bo_sb[:, n:n + 1]), inc="sc")
            p.wait("vector", "sc", t1_done[s][n])
            p.wait("vector", "dout%d" % (n % 4), back(outdma, 4))
            p.wait("vector", "v", back(out_done, 4))
            t2_done[s][n] = p.op("vector", lambda e, s=s, n=n, gg=gg:
                                 e.scalar_tensor_tensor(
                t2b[:, gg % 4, :], accr_sl[:, s % 2, n, :], 0.2, t1b[:, gg % 4, :],
                op0=ALU.mult, op1=ALU.add), inc="v")
            p.wait("vector", "v", t2_done[s][n])
            out_done[s][n] = p.op("vector", lambda e, s=s, n=n, gg=gg: e.tensor_tensor(
                outb[:, gg % 4, :], t2b[:, gg % 4, :], xr_sl[:, s % 2, n, :],
                op=ALU.add), inc="v")
            p.wait("sync", "v", out_done[s][n])
            outdma[s][n] = p.dma("sync", out_d[n * 128:(n + 1) * 128,
                                               s * TS:(s + 1) * TS],
                                 outb[:, gg % 4, :], "dout%d" % (n % 4))
            pb(3)


    for fam in sorted(p.dma_fams):
        p.wait("sync", fam, p.c.get(fam, 0))
    with nc.Block("s4") as blk4:
        p.flush_section(blk4, sems, nc)
    st_C.close()
    st_G.close()
    stk.close()
    return nc


_NC_CACHE = {}


def _get_nc(debug=False):
    key = bool(debug)
    if key not in _NC_CACHE:
        _NC_CACHE[key] = build_nc(debug=key)
    return _NC_CACHE[key]


def kernel(L0_attn_x, L0_attn_Wq, L0_attn_Wkv, L0_attn_Wo, L0_attn_bo,
           L0_attn_mediator, L0_ffn_W_fc, L0_ffn_W_proj, _debug=False):
    x = np.asarray(L0_attn_x, dtype=np.float32)
    wq = np.ascontiguousarray(np.asarray(L0_attn_Wq, np.float32))
    wkv = np.ascontiguousarray(np.asarray(L0_attn_Wkv, np.float32))
    wo = np.ascontiguousarray(np.asarray(L0_attn_Wo, np.float32))
    bo = np.asarray(L0_attn_bo, np.float32)
    med = np.asarray(L0_attn_mediator, np.float32)
    wfct = np.ascontiguousarray(np.asarray(L0_ffn_W_fc, np.float32).T)
    wprojt = np.ascontiguousarray(np.asarray(L0_ffn_W_proj, np.float32).T)
    medt = np.ascontiguousarray(med[0].T)
    bo2d = np.ascontiguousarray(bo.reshape(KC, 128).T)
    ind_c = np.zeros((128, 2), np.float32)
    ind_c[0:64, 0] = 1.0
    ind_c[64:128, 1] = 1.0
    indt_c = np.ascontiguousarray(ind_c.T)
    pones_c = np.zeros((128, 128), np.float32)
    pones_c[0:64, 0:64] = 1.0
    pones_c[64:128, 64:128] = 1.0

    in_maps = []
    for c in range(NCORES):
        b, o = c // 4, (c % 4) * T
        x_fm = np.ascontiguousarray(x[b, o:o + T, :].T)
        in_maps.append({"x_fm": x_fm, "wq": wq, "wkv": wkv, "wo": wo,
                        "wfct": wfct, "wprojt": wprojt, "medt": medt,
                        "bo2d": bo2d, "ind_c": ind_c, "indt_c": indt_c,
                        "pones_c": pones_c})
    nc = _get_nc(debug=_debug)
    res = run_bass_kernel_spmd(nc, in_maps, core_ids=list(range(NCORES)))
    out = np.empty((B, L, D), np.float32)
    for c in range(NCORES):
        b, o = c // 4, (c % 4) * T
        out[b, o:o + T, :] = res.results[c]["out_fm"].T
    if _debug:
        return out, res
    return out


# version 18
# speedup vs baseline: 1.0380x; 1.0380x over previous
import numpy as np
from contextlib import ExitStack

import concourse.bass as bass
import concourse.mybir as mybir
from concourse.bass_utils import run_bass_kernel_spmd

F32 = mybir.dt.float32
BF16 = mybir.dt.bfloat16
AF = mybir.ActivationFunctionType
ALU = mybir.AluOpType
AX = mybir.AxisListType

B, L, D = 2, 8192, 1024
H, DH, M = 16, 64, 64
T = 2048
NS, TS = 4, 512
KC = 8
NP = 8
NLT = 16
FF, NF = 4096, 32
NCORES = 8
SCALE = 0.125
INV_D = 1.0 / 1024.0
LN_EPS = 1e-5
B_SPIKE = [10000.0 / 40951.0, 19000.0 / 40951.0, 28100.0 / 40951.0,
           37100.0 / 40951.0, 44390.0 / 40951.0]

ENGINES = ("sync", "gpsimd", "tensor", "scalar", "vector")


class Plan:

    def __init__(self):
        self.steps = {e: [] for e in ENGINES}
        self.c = {}
        self.dma_fams = set()

    def wait(self, eng, sem, val):
        if val and val > 0:
            self.steps[eng].append(("w", sem, val))

    def inc(self, sem, n=1):
        self.c[sem] = self.c.get(sem, 0) + n
        return self.c[sem]

    def op(self, eng, fn, inc=None, inc_n=1):
        after = None
        if inc is not None:
            after = self.inc(inc, inc_n)
        self.steps[eng].append(("o", fn, inc, inc_n))
        return after

    def dma(self, eng, out, in_, fam):
        self.dma_fams.add(fam)
        return self.op(eng, lambda e, o=out, i=in_: e.dma_start(out=o, in_=i),
                       inc=fam, inc_n=16)

    def mm(self, out, lhsT, rhs, start, stop, inc=None, skip_gc=False):
        return self.op("tensor",
                       lambda e, o=out, l=lhsT, r=rhs, a=start, b=stop, g=skip_gc:
                       e.matmul(o, l, r, start=a, stop=b, skip_group_check=g),
                       inc=inc)

    def barrier(self, bar_name="bar", quiesce=True):
        tgt = self.c.get(bar_name, 0) + len(ENGINES)
        for eng in ENGINES:
            self.inc(bar_name)
            self.steps[eng].append(
                ("o", lambda e, d=self.sems_ref, n=bar_name: e.sem_inc(d[n], 1),
                 None, 1))
        for eng in ENGINES:
            self.wait(eng, bar_name, tgt)
            if quiesce:
                for fam in sorted(self.dma_fams):
                    self.wait(eng, fam, self.c.get(fam, 0))

    def flush_section(self, block, sems, nc=None):
        import contextlib
        cm = nc.allow_low_precision("bf16 softmax-weight operands") if nc \
            else contextlib.nullcontext()
        with cm:
            self._flush(block, sems)

    def _flush(self, block, sems):
        for eng in ENGINES:
            steps = self.steps[eng]
            self.steps[eng] = []

            def body(e, steps=steps):
                watermark = {}
                for st in steps:
                    if st[0] == "w":
                        _, sem, val = st
                        if watermark.get(sem, -1) >= val:
                            continue
                        e.wait_ge(sems[sem], val)
                        watermark[sem] = val
                    else:
                        _, fn, incsem, inc_n = st
                        inst = fn(e)
                        if incsem is not None:
                            inst.then_inc(sems[incsem], inc_n)

            getattr(block, eng)(body)


def build_nc(debug=False, paranoid=0, poison=False):
    nc = bass.Bass("TRN2", target_bir_lowering=False, debug=False)

    x_fm = nc.dram_tensor("x_fm", [D, T], F32, kind="ExternalInput").ap()
    wq_d = nc.dram_tensor("wq", [D, D], F32, kind="ExternalInput").ap()
    wkv_d = nc.dram_tensor("wkv", [D, 2 * D], F32, kind="ExternalInput").ap()
    wo_d = nc.dram_tensor("wo", [D, D], F32, kind="ExternalInput").ap()
    wfct_d = nc.dram_tensor("wfct", [D, FF], F32, kind="ExternalInput").ap()
    wprojt_d = nc.dram_tensor("wprojt", [FF, D], F32, kind="ExternalInput").ap()
    medt_d = nc.dram_tensor("medt", [D, M], F32, kind="ExternalInput").ap()
    bo_d = nc.dram_tensor("bo2d", [128, KC], F32, kind="ExternalInput").ap()
    ind_d = nc.dram_tensor("ind_c", [128, 2], F32, kind="ExternalInput").ap()
    indt_d = nc.dram_tensor("indt_c", [2, 128], F32, kind="ExternalInput").ap()
    pones_d = nc.dram_tensor("pones_c", [128, 128], F32, kind="ExternalInput").ap()
    out_d = nc.dram_tensor("out_fm", [D, T], F32, kind="ExternalOutput").ap()

    xn_scr = nc.dram_tensor("xn_scr", [NS, 128, KC, TS], BF16).ap()
    e2n_scr = nc.dram_tensor("e2n_scr", [NS, 128, NP, TS], BF16).ap()
    acc_scr = nc.dram_tensor("acc_scr", [NS, 128, KC, TS], BF16).ap()
    ccin = nc.dram_tensor("ccin", [129, 1024], F32)
    ccout = nc.dram_tensor("ccout", [129, 1024], F32)

    dbg = {}
    if debug:
        for nm, shp in [("dbg_xn", [128, KC, TS]), ("dbg_k", [128, T]),
                        ("dbg_accT", [128, 1024]), ("dbg_sume", [1, 1024]),
                        ("dbg_mkv", [128, 1024]), ("dbg_g", [128, KC, 1024]),
                        ("dbg_e2n", [128, NP, TS]), ("dbg_q", [128, KC, TS]),
                        ("dbg_y", [128, TS]), ("dbg_acc", [128, KC, TS]),
                        ("dbg_xnf", [128, KC, TS])]:
            dbg[nm] = nc.dram_tensor(nm, shp, F32, kind="ExternalOutput").ap()
        dbg["dbg_h"] = nc.dram_tensor("dbg_h", [128, 4, TS], BF16,
                                      kind="ExternalOutput").ap()

    p = Plan()

    def pb(level=1):
        if level in ("a2a", "a2b", "s2", "ffn", "s3e"):
            p.barrier(quiesce=False)
            return
        if isinstance(paranoid, str):
            if str(level) in paranoid.split(","):
                p.barrier()
        elif paranoid >= (level if isinstance(level, int) else 99):
            p.barrier()
    stk = ExitStack()

    def sb(name, shape, dt):
        return stk.enter_context(nc.sbuf_tensor(name, shape, dt))

    sem_names = ["t", "sc", "v", "gc", "cc", "bar",
                 "dpsn", "dset", "dwkv", "dwq", "dx0", "dx1", "dxnw", "ddbg",
                 "dcc", "dwp", "dwf", "dxq0", "dxq1", "dew", "dcb", "dwo0",
                 "dwo1", "dxnf", "dac0", "dac1", "dld0", "dld1", "dout0",
                 "dout1", "dout2", "dout3"]
    sems = {n: stk.enter_context(nc.semaphore(n)) for n in sem_names}
    p.sems_ref = sems

    ones_bf = sb("ones_bf", [128, 128], BF16)
    med_blk = sb("med_blk", [128, NP, 128], BF16)
    ind = sb("ind", [128, 2], BF16)
    indT = sb("indT", [2, 128], BF16)
    pones = sb("pones", [128, 128], BF16)
    bo_sb = sb("bo_sb", [128, KC], F32)

    psA = stk.enter_context(nc.psum_tensor("psA", [128, 1024], F32))
    psB = stk.enter_context(nc.psum_tensor("psB", [128, 1024], F32))
    psC = stk.enter_context(nc.psum_tensor("psC", [128, 1024], F32))
    psD = stk.enter_context(nc.psum_tensor("psD", [128, 1024], F32))

    st_wq = ExitStack()
    wq_sb = st_wq.enter_context(
        nc.sbuf_tensor("wq_sb", [128, KC, D], BF16, side="right"))
    q_sb = st_wq.enter_context(
        nc.sbuf_tensor("q_sb", [128, KC, T], BF16, side="right"))

    st_A = ExitStack()

    def sbA(name, shape, dt):
        return st_A.enter_context(nc.sbuf_tensor(name, shape, dt))

    wkv_sb = sbA("wkv_sb", [128, KC, 2 * D], BF16)
    xn_sb = sbA("xn_sb", [128, KC, T], BF16)
    k_sb = sbA("k_sb", [128, KC, T], BF16)
    x_sl = sbA("x_sl", [128, 2, KC, TS], BF16)
    x2_sl = sbA("x2_sl", [128, KC, TS], BF16)
    mstat = sbA("mstat", [128, TS], BF16)
    rstat = sbA("rstat", [128, TS], BF16)
    m2f = sbA("m2f", [128, TS], F32)
    varf = sbA("varf", [128, TS], F32)
    rinvf = sbA("rinvf", [128, TS], F32)
    xcb = sbA("xcb", [128, TS], BF16)
    v_rot = sbA("v_rot", [128, 2, 1024], BF16)
    e_rot = sbA("e_rot", [128, 2, 1024], BF16)
    med_stage = sbA("med_stage", [128, KC, M], F32)
    acc_stage = sbA("acc_stage", [128, 1024], F32)
    sume_stage = sbA("sume_stage", [1, 1024], F32)

    poison_done = 0
    if poison:
        with nc.sbuf_tensor("psn", [128, KC * TS], BF16) as psn:
            p.op("gpsimd", lambda e: e.memset(psn[:, :], 1e30), inc="gc")
            for si in range(NS):
                p.dma("gpsimd", xn_scr[si, :, :, :], psn[:, :].rearrange(
                    "p (c t) -> p c t", c=KC), "dpsn")
                p.dma("gpsimd", e2n_scr[si, :, :, :], psn[:, :].rearrange(
                    "p (c t) -> p c t", c=KC), "dpsn")
                poison_done = p.dma("gpsimd", acc_scr[si, :, :, :],
                                    psn[:, :].rearrange("p (c t) -> p c t", c=KC),
                                    "dpsn")
            p.dma("gpsimd", ccin.ap()[0:128, :], psn[:, 0:1024].rearrange(
                "p t -> p t"), "dpsn")
            poison_done = p.dma("gpsimd", ccout.ap()[0:128, :],
                                psn[:, 0:1024].rearrange("p t -> p t"), "dpsn")
    g_med = p.dma("gpsimd", med_stage[:, :, :],
                  medt_d.rearrange("(c p) m -> p c m", p=128), "dset")
    g_bo = p.dma("gpsimd", bo_sb[:, :], bo_d[:, :], "dset")
    p.op("gpsimd", lambda e: e.memset(ones_bf[:, :], 1.0), inc="gc")
    gc_setup = p.op("gpsimd", lambda e: e.memset(med_blk[:, :, :], 0.0), inc="gc")
    p.dma("gpsimd", ind[:, :], ind_d, "dset")
    p.dma("gpsimd", indT[:, :], indt_d, "dset")
    g_ind = p.dma("gpsimd", pones[:, :], pones_d, "dset")

    p.wait("scalar", "dset", g_ind)
    p.wait("scalar", "gc", gc_setup)
    med_ready = None
    for pp in range(NP):
        p.op("scalar", lambda e, pp=pp: e.activation(
            med_blk[0:64, pp, 0:64], med_stage[0:64, pp, :], AF.Copy, scale=SCALE))
        med_ready = p.op("scalar", lambda e, pp=pp: e.activation(
            med_blk[64:128, pp, 64:128], med_stage[64:128, pp, :], AF.Copy,
            scale=SCALE), inc="sc")

    _weights_pending = True

    xs_ready = [None] * NS
    x2_ready = [None] * NS
    stats_done = [None] * NS
    ln_v_done = [None] * NS
    rstd_done = [None] * NS
    xn_done = [None] * NS
    xnscr_done = [None] * NS
    p.wait("tensor", "gc", gc_setup)
    p.wait("tensor", "dset", g_ind)

    def issue_x(s):
        if s >= 2:
            p.wait("gpsimd", "v", xn_done[s - 2])
        xs_ready[s] = p.dma(
            "gpsimd", x_sl[:, s % 2, :, :],
            x_fm.rearrange("(c p) t -> p c t", p=128)[:, :, s * TS:(s + 1) * TS],
            "dx%d" % (s % 2))

    issue_x(0)
    issue_x(1)
    wkv_ready = p.dma("gpsimd", wkv_sb[:, :, :],
                      wkv_d.rearrange("(c p) n -> p c n", p=128), "dwkv")
    wq_ready = p.dma("gpsimd", wq_sb[:, :, :],
                     wq_d.rearrange("(c p) n -> p c n", p=128), "dwq")

    def a1_slice(s):
        if xs_ready[s] is None:
            issue_x(s)
        p.wait("scalar", "dx%d" % (s % 2), xs_ready[s])
        if s >= 1:
            p.wait("scalar", "t", stats_done[s - 1])
        x2_ready[s] = p.op("scalar", lambda e, s=s: e.activation(
            x2_sl[:, :, :], x_sl[:, s % 2, :, :], AF.Square), inc="sc")
        p.wait("tensor", "sc", x2_ready[s])
        if s >= 1:
            p.wait("tensor", "v", ln_v_done[s - 1])
        for kk in range(KC):
            p.mm(psD[:, 0:TS], ones_bf[:, :], x_sl[:, s % 2, kk, :],
                 start=(kk == 0), stop=(kk == KC - 1))
        for kk in range(KC):
            stats_done[s] = p.mm(psD[:, TS:2 * TS], ones_bf[:, :], x2_sl[:, kk, :],
                                 start=(kk == 0), stop=(kk == KC - 1),
                                 inc="t" if kk == KC - 1 else None)
        p.wait("vector", "t", stats_done[s])
        if s >= 1:
            p.wait("vector", "v", xn_done[s - 1])
        c1 = p.op("vector", lambda e: e.tensor_scalar_mul(
            mstat[:, :], psD[:, 0:TS], INV_D), inc="v")
        p.wait("vector", "v", c1)
        c2 = p.op("vector", lambda e: e.tensor_tensor(
            m2f[:, :], mstat[:, :], mstat[:, :], op=ALU.mult), inc="v")
        p.wait("vector", "v", c2)
        c3 = p.op("vector", lambda e: e.scalar_tensor_tensor(
            varf[:, :], psD[:, TS:2 * TS], INV_D, m2f[:, :],
            op0=ALU.mult, op1=ALU.subtract), inc="v")
        p.wait("vector", "v", c3)
        c4 = p.op("vector", lambda e: e.tensor_scalar_add(
            varf[:, :], varf[:, :], LN_EPS), inc="v")
        p.wait("vector", "v", c4)
        ln_v_done[s] = p.op("vector", lambda e: e.reciprocal(rinvf[:, :], varf[:, :]),
                            inc="v")
        p.wait("scalar", "v", ln_v_done[s])
        rstd_done[s] = p.op("scalar", lambda e: e.activation(
            rstat[:, :], rinvf[:, :], AF.Sqrt), inc="sc")
        p.wait("vector", "sc", rstd_done[s])
        if s >= 1:
            p.wait("vector", "dxnw", xnscr_done[s - 1])
        prev_mul = None
        for kk in range(KC):
            if prev_mul is not None:
                p.wait("vector", "v", prev_mul)
            ck = p.op("vector", lambda e, s=s, kk=kk: e.tensor_tensor(
                xcb[:, :], x_sl[:, s % 2, kk, :], mstat[:, :], op=ALU.subtract),
                inc="v")
            p.wait("vector", "v", ck)
            xn_done[s] = prev_mul = p.op(
                "vector", lambda e, s=s, kk=kk: e.tensor_tensor(
                    xn_sb[:, kk, s * TS:(s + 1) * TS], xcb[:, :], rstat[:, :],
                    op=ALU.mult), inc="v")
        p.wait("sync", "v", xn_done[s])
        p.wait("sync", "dpsn", poison_done)
        xnscr_done[s] = p.dma("sync", xn_scr[s, :, :, :], fam="dxnw", in_=
                              xn_sb[:, :, s * TS:(s + 1) * TS])

    if debug:
        p.wait("gpsimd", "v", xn_done[0])
        p.dma("gpsimd", dbg["dbg_xn"], xn_sb[:, :, 0:TS], "ddbg")

    pb("a2a")
    pK6 = [psA[:, 0:TS], psA[:, TS:2 * TS], psB[:, 0:TS], psB[:, TS:2 * TS],
           psC[:, 0:TS], psC[:, TS:2 * TS]]
    k_evac = [[None] * KC for _ in range(NS)]
    q_evac = [[None] * KC for _ in range(NS)]
    ring_evac = []
    k_grp = 0

    def proj_slice(s4, w_sb, w_sem, w_val, dst, evac):
        nonlocal k_grp
        p.wait("tensor", "v", xn_done[s4])
        p.wait("tensor", w_sem, w_val)
        for n in range(KC):
            tile = pK6[k_grp % 6]
            prev = k_grp - 6
            if prev >= 0:
                p.wait("tensor", "sc", ring_evac[prev])
            kmm = None
            for kk in range(KC):
                kmm = p.mm(tile, w_sb[:, kk, n * 128:(n + 1) * 128],
                           xn_sb[:, kk, s4 * TS:(s4 + 1) * TS],
                           start=(kk == 0), stop=(kk == KC - 1),
                           inc="t" if kk == KC - 1 else None)
            p.wait("scalar", "t", kmm)
            evac[s4][n] = p.op("scalar", lambda e, n=n, s4=s4, tile=tile, d=dst:
                               e.copy(d[:, n, s4 * TS:(s4 + 1) * TS], tile),
                               inc="sc")
            ring_evac.append(evac[s4][n])
            k_grp += 1

    def a2a_slice(s4):
        proj_slice(s4, wkv_sb, "dwkv", wkv_ready, k_sb, k_evac)

    def a2q_slice(s4):
        proj_slice(s4, wq_sb, "dwq", wq_ready, q_sb, q_evac)

    a1_slice(0)
    for _s in range(1, NS):
        a1_slice(_s)
        a2a_slice(_s - 1)
        a2q_slice(_s - 1)
    a2a_slice(NS - 1)
    a2q_slice(NS - 1)

    if debug:
        p.wait("gpsimd", "sc", k_evac[NS - 1][0])
        p.dma("gpsimd", dbg["dbg_k"], k_sb[:, 0, :], "ddbg")

    pb("a2b")
    p.wait("vector", "sc", ring_evac[-1])
    p.wait("tensor", "sc", ring_evac[-1])
    p.op("vector", lambda e: e.memset(psC[:, :], 0.0))
    accz = p.op("vector", lambda e: e.memset(psD[:, :], 0.0), inc="v")
    p.wait("tensor", "v", accz)
    p.wait("tensor", "sc", med_ready)
    v_mm = [None] * NLT
    s_mm = [None] * NLT
    v_evac = [None] * NLT
    e_done = [None] * NLT
    acc_prog = [None] * NLT
    for lt in range(NLT):
        if lt >= 1:
            p.wait("tensor", "sc", e_done[lt - 1])
        for pp in range(NP):
            s_mm[lt] = p.mm(psB[:, pp * 128:(pp + 1) * 128],
                            k_sb[:, pp, lt * 128:(lt + 1) * 128],
                            med_blk[:, pp, :], start=True, stop=True,
                            inc="t" if pp == NP - 1 else None)
        p.wait("scalar", "t", s_mm[lt])
        e_done[lt] = p.op("scalar", lambda e, lt=lt: e.activation(
            e_rot[:, lt % 2, :], psB[:, :], AF.Exp), inc="sc")
        if lt >= 1:
            p.wait("tensor", "v", v_evac[lt - 1])
        for kk in range(KC):
            lhs = xn_sb[:, kk, lt * 128:(lt + 1) * 128]
            p.mm(psA[:, 0:512], lhs, wkv_sb[:, kk, 1024:1536],
                 start=(kk == 0), stop=(kk == KC - 1))
            v_mm[lt] = p.mm(psA[:, 512:1024], lhs, wkv_sb[:, kk, 1536:2048],
                            start=(kk == 0), stop=(kk == KC - 1),
                            inc="t" if kk == KC - 1 else None)
        p.wait("vector", "t", v_mm[lt])
        v_evac[lt] = p.op("vector", lambda e, lt=lt: e.tensor_copy(
            v_rot[:, lt % 2, :], psA[:, :]), inc="v")
        p.wait("tensor", "sc", e_done[lt])
        p.wait("tensor", "v", v_evac[lt])
        for pp in range(NP):
            p.mm(psC[:, pp * 128:(pp + 1) * 128], v_rot[:, lt % 2, pp * 128:(pp + 1) * 128],
                 e_rot[:, lt % 2, pp * 128:(pp + 1) * 128],
                 start=False, stop=(lt == NLT - 1), skip_gc=True)
        for pp in range(NP):
            acc_prog[lt] = p.mm(psD[0:1, pp * 128:(pp + 1) * 128], ones_bf[:, 0:1],
                                e_rot[:, lt % 2, pp * 128:(pp + 1) * 128],
                                start=False, stop=(lt == NLT - 1),
                                inc="t" if pp == NP - 1 else None, skip_gc=True)

    p.wait("scalar", "t", acc_prog[NLT - 1])
    p.op("scalar", lambda e: e.copy(acc_stage[:, :], psC[:, :]))
    stage_done = p.op("scalar", lambda e: e.copy(sume_stage[:, :], psD[0:1, :]),
                      inc="sc")
    p.wait("gpsimd", "sc", stage_done)
    g1 = p.dma("gpsimd", ccin.ap()[0:128, :], acc_stage[:, :], "dcc")
    g2 = p.dma("gpsimd", ccin.ap()[128:129, :], sume_stage[:, :], "dcc")


    if debug:
        p.wait("sync", "sc", stage_done)
        p.dma("sync", dbg["dbg_accT"], acc_stage[:, :], "ddbg")
        p.dma("sync", dbg["dbg_sume"], sume_stage[:, :], "ddbg")

    p.barrier()
    with nc.Block("s1") as blk1:
        p.flush_section(blk1, sems, nc)
    st_A.close()

    st_W2 = ExitStack()
    wprojt_sb = st_W2.enter_context(nc.sbuf_tensor("wprojt_sb", [128, NF, D], BF16))
    wfct_sb = st_W2.enter_context(nc.sbuf_tensor("wfct_sb", [128, KC, FF], BF16))
    st_S2 = ExitStack()

    def sb2(name, shape, dt):
        return st_S2.enter_context(nc.sbuf_tensor(name, shape, dt))

    e2_sb = sb2("e2_sb", [128, NP, TS], BF16)
    e2n_sb = sb2("e2n_sb", [128, NP, TS], BF16)
    recb = sb2("recb", [128, 2, TS], BF16)

    p.wait("gpsimd", "dcc", g2)
    p.op("gpsimd", lambda e: e.collective_compute(
        "AllReduce", ALU.add, replica_groups=[[0, 1, 2, 3], [4, 5, 6, 7]],
        ins=[ccin.ap().opt()], outs=[ccout.ap().opt()]), inc="cc")

    wproj_ready = None
    for g4 in range(4):
        wproj_ready = p.dma(
            "gpsimd", wprojt_sb[:, g4 * 8:(g4 + 1) * 8, :],
            wprojt_d.rearrange("(c p) n -> p c n", p=128)[:, g4 * 8:(g4 + 1) * 8, :],
            "dwp")
    wfct_ready = [None] * KC
    for g8 in range(KC):
        wfct_ready[g8] = p.dma(
            "gpsimd", wfct_sb[:, :, g8 * 512:(g8 + 1) * 512],
            wfct_d.rearrange("(c p) f -> p c f", p=128)[:, :, g8 * 512:(g8 + 1) * 512],
            "dwf")

    s2_mm = [[None] * NP for _ in range(NS)]
    e2_done = [[None] * NP for _ in range(NS)]
    sum_mm = [[None] * NP for _ in range(NS)]
    rec_done = [[None] * NP for _ in range(NS)]
    bc_mm = [[None] * NP for _ in range(NS)]
    e2n_done = [[None] * NP for _ in range(NS)]
    e2nscr_done = [None] * NS
    if True:
        s2ring = [psB[:, 0:TS], psB[:, TS:2 * TS], psD[:, 0:TS], psD[:, TS:2 * TS]]
        smring = [psC[:, 0:TS], psC[:, TS:2 * TS], psA[:, 0:TS], psA[:, TS:2 * TS]]

        def em_s2(s, pp):
            g = s * NP + pp
            p.wait("tensor", "sc", q_evac[s][pp])
            p.wait("tensor", "sc", med_ready)
            if g >= 4:
                p.wait("tensor", "sc", e2_done[(g - 4) // NP][(g - 4) % NP])
            s2_mm[s][pp] = p.mm(s2ring[g % 4],
                                med_blk[:, pp, :], q_sb[:, pp, s * TS:(s + 1) * TS],
                                start=True, stop=True, inc="t")

        def em_exp(s, pp):
            g = s * NP + pp
            p.wait("scalar", "t", s2_mm[s][pp])
            if s > 0:
                p.wait("scalar", "t", bc_mm[s - 1][pp])
                p.wait("scalar", "v", e2n_done[s - 1][pp])
            e2_done[s][pp] = p.op("scalar", lambda e, s=s, pp=pp, g=g: e.activation(
                e2_sb[:, pp, :], s2ring[g % 4], AF.Exp),
                inc="sc")

        def em_sumb(s, pp):
            g = s * NP + pp
            p.wait("tensor", "sc", e2_done[s][pp])
            p.wait("tensor", "gc", gc_setup)
            if g >= 4:
                p.wait("tensor", "v", e2n_done[(g - 4) // NP][(g - 4) % NP])
            bc_mm[s][pp] = p.mm(smring[g % 4],
                                pones[:, :], e2_sb[:, pp, :], start=True, stop=True,
                                inc="t")

        def em_div(s, pp):
            g = s * NP + pp
            p.wait("vector", "t", bc_mm[s][pp])
            rg = p.op("vector", lambda e, pp=pp, g=g: e.reciprocal(
                recb[:, pp % 2, :], smring[g % 4]),
                inc="v")
            p.wait("vector", "v", rg)
            p.wait("vector", "sc", e2_done[s][pp])
            if s >= 1:
                p.wait("vector", "dew", e2nscr_done[s - 1])
            e2n_done[s][pp] = p.op("vector", lambda e, s=s, pp=pp: e.tensor_tensor(
                e2n_sb[:, pp, :], e2_sb[:, pp, :],
                recb[:, pp % 2, :], op=ALU.mult), inc="v")

    def s2_qm_tail(s):
        p.wait("sync", "v", e2n_done[s][NP - 1])
        e2nscr_done[s] = p.dma("sync", e2n_scr[s, :, :, :], e2n_sb[:, :, :],
                               "dew")
        if debug and s == 0:
            p.wait("gpsimd", "v", e2n_done[0][NP - 1])
            p.dma("gpsimd", dbg["dbg_e2n"], e2n_sb[:, :, :], "ddbg")
            p.wait("gpsimd", "sc", q_evac[0][KC - 1])
            dq = p.dma("gpsimd", dbg["dbg_q"], q_sb[:, :, 0:TS], "ddbg")
            p.wait("scalar", "ddbg", dq)
            p.wait("vector", "ddbg", dq)

    UNITS = NS * NP

    def unit(g):
        return g // NP, g % NP

    for g in range(UNITS + 2):
        if g < UNITS:
            em_s2(*unit(g))
            em_exp(*unit(g))
        if g >= 2:
            su, pu = unit(g - 2)
            em_sumb(su, pu)
            em_div(su, pu)
            if pu == NP - 1:
                s2_qm_tail(su)

    p.barrier()
    with nc.Block("s2") as blk2:
        p.flush_section(blk2, sems, nc)
    st_S2.close()
    st_wq.close()

    st_G = ExitStack()
    G_sb = st_G.enter_context(nc.sbuf_tensor("G_sb", [128, KC, D], BF16,
                                             side="right"))
    mkv_sb = st_G.enter_context(nc.sbuf_tensor("mkv_sb", [128, 1024], BF16,
                                               side="right"))
    ccb_acc = st_G.enter_context(nc.sbuf_tensor("ccb_acc", [128, 1024], BF16,
                                                side="right"))
    ccb_sume = st_G.enter_context(nc.sbuf_tensor("ccb_sume", [1, 1024], BF16,
                                                 side="right"))
    srecip = st_G.enter_context(nc.sbuf_tensor("srecip", [128, 1024], BF16,
                                               side="right"))
    st_F = ExitStack()

    def sbF(name, shape, dt):
        return st_F.enter_context(nc.sbuf_tensor(name, shape, dt))

    wo_rot = sbF("wo_rot", [128, 2, D], BF16)
    xn_f = sbF("xn_f", [128, KC, TS], BF16)
    h_sb = sbF("h_sb", [128, NF, TS], BF16)
    hrelu = sbF("hrelu", [128, TS], BF16)
    accs_sb = sbF("accs_sb", [128, 2, TS], BF16)
    sp1 = sbF("sp1", [128, TS], BF16)
    sp2 = sbF("sp2", [128, TS], BF16)
    dbgy_buf = sbF("dbgy_buf", [128, TS], F32) if debug else None

    p.wait("gpsimd", "cc", 1)
    p.dma("gpsimd", ccb_acc[:, :], ccout.ap()[0:128, :], "dcb")
    ccback = p.dma("gpsimd", ccb_sume[:, :], ccout.ap()[128:129, :], "dcb")
    g_mm = [None] * H
    g_evac = [None] * H
    g_state = {}
    sr_state = {}

    def prelude_b(relu31, hmul31):
        p.wait("tensor", "dcb", ccback)
        p.mm(psD[:, 0:512], ones_bf[0:1, :], ccb_sume[0:1, 0:512],
             start=True, stop=True)
        bc2 = p.mm(psD[:, 512:1024], ones_bf[0:1, :], ccb_sume[0:1, 512:1024],
                   start=True, stop=True, inc="t")
        p.wait("vector", "t", bc2)
        sr_done = p.op("vector", lambda e: e.reciprocal(srecip[:, :], psD[:, :]),
                       inc="v")
        sr_state["sr"] = sr_done
        p.wait("vector", "v", sr_done)
        mkv_done = p.op("vector", lambda e: e.tensor_tensor(
            mkv_sb[:, :], ccb_acc[:, :], srecip[:, :], op=ALU.mult), inc="v")
        p.wait("tensor", "sc", relu31)
        p.wait("tensor", "v", hmul31)
        for h in range(H):
            pp, off = h // 2, 64 * (h % 2)
            if h >= 2:
                p.wait("gpsimd", "t", g_mm[h - 2])
            woh = p.dma("gpsimd", wo_rot[off:off + 64, h % 2, :],
                        wo_d[h * 64:(h + 1) * 64, :], "dwo%d" % (h % 2))
            p.wait("tensor", "v", mkv_done)
            p.wait("tensor", "dwo%d" % (h % 2), woh)
            if h >= 2:
                p.wait("tensor", "sc", g_evac[h - 2])
            ps = psB if h % 2 == 0 else psC
            lhs = mkv_sb[off:off + 64, pp * 128 + off:pp * 128 + off + 64]
            p.mm(ps[0:64, 0:512], lhs, wo_rot[off:off + 64, h % 2, 0:512],
                 start=True, stop=True)
            g_mm[h] = p.mm(ps[0:64, 512:1024], lhs,
                           wo_rot[off:off + 64, h % 2, 512:1024],
                           start=True, stop=True, inc="t")
            p.wait("scalar", "t", g_mm[h])
            g_evac[h] = p.op("scalar", lambda e, h=h, pp=pp, off=off, ps=ps: e.copy(
                G_sb[off:off + 64, pp, :], ps[0:64, :]), inc="sc")
        g_state["g_all"] = g_evac[H - 1]
        if debug:
            p.wait("gpsimd", "v", mkv_done)
            p.dma("gpsimd", dbg["dbg_mkv"], mkv_sb[:, :], "ddbg")
            p.wait("gpsimd", "sc", g_state["g_all"])
            p.dma("gpsimd", dbg["dbg_g"], G_sb[:, :, :], "ddbg")

    xnf_ready = [None] * NS
    fc_all = [None] * NS
    hmul = [[None] * NF for _ in range(NS)]
    relu_done = [[None] * NF for _ in range(NS)]
    proj_mm = [[None] * KC for _ in range(NS)]
    spike_done = [[None] * KC for _ in range(NS)]
    accdma = [[None] * KC for _ in range(NS)]
    pH = [psA[:, 0:512], psA[:, 512:1024], psB[:, 0:512], psB[:, 512:1024]]
    for s in range(NS):
        p.wait("sync", "dxnw", xnscr_done[NS - 1])
        if s >= 1:
            p.wait("sync", "t", fc_all[s - 1])
        xnf_ready[s] = p.dma("sync", xn_f[:, :, :], xn_scr[s, :, :, :], "dxnf")
        if debug and s == 0:
            p.wait("gpsimd", "dxnf", xnf_ready[0])
            dxnf = p.dma("gpsimd", dbg["dbg_xnf"], xn_f[:, :, :], "ddbg")
        if debug and s == 1:
            p.wait("sync", "ddbg", dxnf)
        for g2 in range(NF // 2):
            f0 = 2 * g2
            p.wait("tensor", "dwf", wfct_ready[KC - 1])
            if g2 >= 2:
                p.wait("tensor", "v", hmul[s][2 * (g2 - 2) + 1])
            elif s >= 1:
                p.wait("tensor", "v", hmul[s - 1][2 * (g2 - 2 + NF // 2) + 1])
            p.wait("tensor", "dxnf", xnf_ready[s])
            for kk in range(KC):
                for fi in range(2):
                    f = f0 + fi
                    fc_all[s] = p.mm(pH[2 * (g2 % 2) + fi],
                                     wfct_sb[:, kk, f * 128:(f + 1) * 128],
                                     xn_f[:, kk, :], start=(kk == 0),
                                     stop=(kk == KC - 1),
                                     inc="t" if (kk == KC - 1 and fi == 1) else None)
            for fi in range(2):
                f = f0 + fi
                p.wait("scalar", "t", fc_all[s])
                if fi == 1:
                    p.wait("scalar", "v", hmul[s][f - 1])
                elif g2 >= 1:
                    p.wait("scalar", "v", hmul[s][f0 - 1])
                elif s >= 1:
                    p.wait("scalar", "v", hmul[s - 1][NF - 1])
                relu_done[s][f] = p.op(
                    "scalar", lambda e, ps=pH[2 * (g2 % 2) + fi]: e.activation(
                        hrelu[:, :], ps, AF.Relu), inc="sc")
                p.wait("vector", "sc", relu_done[s][f])
                hmul[s][f] = p.op(
                    "vector", lambda e, s=s, f=f, ps=pH[2 * (g2 % 2) + fi]:
                    e.tensor_tensor(h_sb[:, f, :], ps, hrelu[:, :], op=ALU.mult),
                    inc="v")
        pb(2)
        if s == 0:
            prelude_b(relu_done[0][NF - 1], hmul[0][NF - 1])
        pyring = [psC[:, 0:TS], psC[:, TS:2 * TS], psD[:, 0:TS], psD[:, TS:2 * TS]]
        for n in range(KC):
            gg = s * KC + n
            p.wait("tensor", "v", hmul[s][NF - 1])
            p.wait("tensor", "dwp", wproj_ready)
            if s == 0 and n == 0:
                p.wait("tensor", "sc", g_state["g_all"])
            if s == 0 and n == 2:
                p.wait("tensor", "v", sr_state["sr"])
            if gg >= 4:
                pv = spike_done[(gg - 4) // KC][(gg - 4) % KC]
                p.wait("tensor", "v", pv)
            for ff in range(NF):
                proj_mm[s][n] = p.mm(pyring[gg % 4],
                                     wprojt_sb[:, ff, n * 128:(n + 1) * 128],
                                     h_sb[:, ff, :], start=(ff == 0),
                                     stop=(ff == NF - 1),
                                     inc="t" if ff == NF - 1 else None)
            p.wait("vector", "t", proj_mm[s][n])
            if debug and s == 0 and n == 0:
                dyv = p.op("vector", lambda e: e.tensor_copy(dbgy_buf[:, :],
                                                             psC[:, 0:TS]), inc="v")
                p.wait("sync", "v", dyv)
                p.dma("sync", dbg["dbg_y"], dbgy_buf[:, :], "ddbg")
                p.wait("sync", "v", hmul[0][NF - 1])
                for _fc in range(4):
                    p.dma("sync", dbg["dbg_h"][:, _fc, :], h_sb[:, _fc, :], "ddbg")
            if s + n > 0:
                pv = accdma[s][n - 2] if n >= 2 else accdma[s - 1][KC - 2 + n]
                p.wait("vector", "dac%d" % (n % 2), pv)
            yps = pyring[gg % 4]
            ck = p.op("vector", lambda e, yps=yps: e.tensor_single_scalar(
                sp1[:, :], yps, B_SPIKE[0], op=ALU.is_gt), inc="v")
            for bi, (dst, src) in enumerate([(sp2, sp1), (sp1, sp2), (sp2, sp1)]):
                p.wait("vector", "v", ck)
                ck = p.op("vector", lambda e, yps=yps, d=dst, r=src, b=B_SPIKE[bi + 1]:
                          e.scalar_tensor_tensor(d[:, :], yps, b, r[:, :],
                                                 op0=ALU.is_gt, op1=ALU.add), inc="v")
            p.wait("vector", "v", ck)
            spike_done[s][n] = p.op(
                "vector", lambda e, yps=yps, n=n: e.scalar_tensor_tensor(
                    accs_sb[:, n % 2, :], yps, B_SPIKE[4], sp2[:, :],
                    op0=ALU.is_gt, op1=ALU.add), inc="v")
            p.wait("gpsimd", "v", spike_done[s][n])
            accdma[s][n] = p.dma("gpsimd", acc_scr[s, :, n, :],
                                 accs_sb[:, n % 2, :], "dac%d" % (n % 2))
            pb(3)
    pb("s3e")
    p.barrier()
    with nc.Block("s3") as blk3:
        p.flush_section(blk3, sems, nc)
    st_F.close()
    st_W2.close()

    st_C = ExitStack()

    def sbC(name, shape, dt):
        return st_C.enter_context(nc.sbuf_tensor(name, shape, dt))

    e2n_sl = sbC("e2n_sl", [128, 2, NP, TS], BF16)
    xr_sl = sbC("xr_sl", [128, 2, KC, TS], F32)
    accr_sl = sbC("accr_sl", [128, 2, KC, TS], BF16)
    t1b = sbC("t1b", [128, 4, TS], F32)
    t2b = sbC("t2b", [128, 4, TS], F32)
    outb = sbC("outb", [128, 4, TS], F32)

    attn_mm = [[None] * KC for _ in range(NS)]
    t1_done = [[None] * KC for _ in range(NS)]
    t2_done = [[None] * KC for _ in range(NS)]
    out_done = [[None] * KC for _ in range(NS)]
    outdma = [[None] * KC for _ in range(NS)]
    cin = [None] * NS
    p.wait("sync", "dew", e2nscr_done[NS - 1])

    def s4_loads(u):
        if u >= 2:
            p.wait("sync", "t", attn_mm[u - 2][KC - 1])
            p.wait("sync", "v", out_done[u - 2][KC - 1])
        p.wait("sync", "dac0", accdma[u][KC - 2])
        p.wait("sync", "dac1", accdma[u][KC - 1])
        fam = "dld%d" % (u % 2)
        p.dma("sync", e2n_sl[:, u % 2, :, :], e2n_scr[u, :, :, :], fam)
        p.dma("sync", xr_sl[:, u % 2, :, :],
              x_fm.rearrange("(c p) t -> p c t", p=128)[:, :, u * TS:(u + 1) * TS],
              fam)
        cin[u] = p.dma("sync", accr_sl[:, u % 2, :, :], acc_scr[u, :, :, :], fam)

    for s in range(NS):
        if s == 0:
            s4_loads(0)
        if s + 1 < NS:
            s4_loads(s + 1)
        if debug and s == 0:
            p.wait("gpsimd", "dld0", cin[0])
            dacc = p.dma("gpsimd", dbg["dbg_acc"], accr_sl[:, 0, :, :], "ddbg")
            p.wait("sync", "ddbg", dacc)
        p.wait("tensor", "dld%d" % (s % 2), cin[s])
        aring = [psA[:, 0:TS], psA[:, TS:2 * TS], psB[:, 0:TS], psB[:, TS:2 * TS]]
        for n in range(KC):
            gg = s * KC + n

            def back(arr, k):
                return arr[(gg - k) // KC][(gg - k) % KC] if gg >= k else None
            p.wait("tensor", "sc", back(t1_done, 4))
            for kk in range(KC):
                attn_mm[s][n] = p.mm(aring[gg % 4],
                                     G_sb[:, kk, n * 128:(n + 1) * 128],
                                     e2n_sl[:, s % 2, kk, :], start=(kk == 0),
                                     stop=(kk == KC - 1),
                                     inc="t" if kk == KC - 1 else None)
            p.wait("scalar", "t", attn_mm[s][n])
            p.wait("scalar", "v", back(t2_done, 4))
            t1_done[s][n] = p.op("scalar", lambda e, s=s, n=n, gg=gg: e.activation(
                t1b[:, gg % 4, :], aring[gg % 4], AF.Identity,
                bias=bo_sb[:, n:n + 1]), inc="sc")
            p.wait("vector", "sc", t1_done[s][n])
            p.wait("vector", "dout%d" % (n % 4), back(outdma, 4))
            p.wait("vector", "v", back(out_done, 4))
            t2_done[s][n] = p.op("vector", lambda e, s=s, n=n, gg=gg:
                                 e.scalar_tensor_tensor(
                t2b[:, gg % 4, :], accr_sl[:, s % 2, n, :], 0.2, t1b[:, gg % 4, :],
                op0=ALU.mult, op1=ALU.add), inc="v")
            p.wait("vector", "v", t2_done[s][n])
            out_done[s][n] = p.op("vector", lambda e, s=s, n=n, gg=gg: e.tensor_tensor(
                outb[:, gg % 4, :], t2b[:, gg % 4, :], xr_sl[:, s % 2, n, :],
                op=ALU.add), inc="v")
            p.wait("sync", "v", out_done[s][n])
            outdma[s][n] = p.dma("sync", out_d[n * 128:(n + 1) * 128,
                                               s * TS:(s + 1) * TS],
                                 outb[:, gg % 4, :], "dout%d" % (n % 4))
            pb(3)


    for fam in sorted(p.dma_fams):
        p.wait("sync", fam, p.c.get(fam, 0))
    with nc.Block("s4") as blk4:
        p.flush_section(blk4, sems, nc)
    st_C.close()
    st_G.close()
    stk.close()
    return nc


_NC_CACHE = {}


def _get_nc(debug=False):
    key = bool(debug)
    if key not in _NC_CACHE:
        _NC_CACHE[key] = build_nc(debug=key)
    return _NC_CACHE[key]


def kernel(L0_attn_x, L0_attn_Wq, L0_attn_Wkv, L0_attn_Wo, L0_attn_bo,
           L0_attn_mediator, L0_ffn_W_fc, L0_ffn_W_proj, _debug=False):
    x = np.asarray(L0_attn_x, dtype=np.float32)
    wq = np.ascontiguousarray(np.asarray(L0_attn_Wq, np.float32))
    wkv = np.ascontiguousarray(np.asarray(L0_attn_Wkv, np.float32))
    wo = np.ascontiguousarray(np.asarray(L0_attn_Wo, np.float32))
    bo = np.asarray(L0_attn_bo, np.float32)
    med = np.asarray(L0_attn_mediator, np.float32)
    wfct = np.ascontiguousarray(np.asarray(L0_ffn_W_fc, np.float32).T)
    wprojt = np.ascontiguousarray(np.asarray(L0_ffn_W_proj, np.float32).T)
    medt = np.ascontiguousarray(med[0].T)
    bo2d = np.ascontiguousarray(bo.reshape(KC, 128).T)
    ind_c = np.zeros((128, 2), np.float32)
    ind_c[0:64, 0] = 1.0
    ind_c[64:128, 1] = 1.0
    indt_c = np.ascontiguousarray(ind_c.T)
    pones_c = np.zeros((128, 128), np.float32)
    pones_c[0:64, 0:64] = 1.0
    pones_c[64:128, 64:128] = 1.0

    in_maps = []
    for c in range(NCORES):
        b, o = c // 4, (c % 4) * T
        x_fm = np.ascontiguousarray(x[b, o:o + T, :].T)
        in_maps.append({"x_fm": x_fm, "wq": wq, "wkv": wkv, "wo": wo,
                        "wfct": wfct, "wprojt": wprojt, "medt": medt,
                        "bo2d": bo2d, "ind_c": ind_c, "indt_c": indt_c,
                        "pones_c": pones_c})
    nc = _get_nc(debug=_debug)
    res = run_bass_kernel_spmd(nc, in_maps, core_ids=list(range(NCORES)))
    out = np.empty((B, L, D), np.float32)
    for c in range(NCORES):
        b, o = c // 4, (c % 4) * T
        out[b, o:o + T, :] = res.results[c]["out_fm"].T
    if _debug:
        return out, res
    return out
